# version 01
# baseline (speedup 1.0000x reference)
import numpy as np
import ml_dtypes
import concourse.bass as bass
import concourse.mybir as mybir
from concourse.bass_utils import run_bass_kernel_spmd
from contextlib import ExitStack

F32 = mybir.dt.float32
BF16 = mybir.dt.bfloat16
I8 = mybir.dt.int8
ALU = mybir.AluOpType
ACT = mybir.ActivationFunctionType
AX = mybir.AxisListType.X

B, S, H, FF, II = 4, 4096, 1024, 4096, 2816
T = 2048
NHB, NFB, NIB = 8, 32, 22
NCH, CHT = 8, 256
LR = 1e-3
EPS = 1e-5
MASK_NEG = -1.0e9
LOSS_SCALE = 1.0 / (4 * 512 * 1024)
_BF = ml_dtypes.bfloat16
ENGINES = ("sync", "tensor", "vector", "scalar", "gpsimd")


def own_blocks(p):
    return [j for j in range(32) if ((j % 4) in (0, 3)) == (p == 0)]

OWN = {0: own_blocks(0), 1: own_blocks(1)}


class Sem:
    __slots__ = ("h", "n", "name")

    def __init__(self, h, name):
        self.h, self.n, self.name = h, 0, name


class Prog:
    def __init__(self, nc):
        self.nc = nc
        self.st = {e: [] for e in ENGINES}

    def emit(self, eng, fn, waits=(), incs=()):
        tgts = tuple((s.h, int(t)) for (s, t) in waits if t > 0)
        bumped = []
        for (s, amt) in incs:
            s.n += amt
            bumped.append((s.h, amt))
        bumped = tuple(bumped)

        def run(e, fn=fn, tgts=tgts, bumped=bumped):
            for (h, t) in tgts:
                e.wait_ge(h, t)
            inst = fn(e)
            for (h, amt) in bumped:
                inst = inst.then_inc(h, amt)
        self.st[eng].append(run)

    def drain(self, eng):
        self.st[eng].append(lambda e: e.drain())

    def waits(self, eng, waits):
        tgts = tuple((s.h, int(t)) for (s, t) in waits if t > 0)
        if tgts:
            def run(e, tgts=tgts):
                for (h, t) in tgts:
                    e.wait_ge(h, t)
            self.st[eng].append(run)


def build_program(upto=4, dbg=None, nq=16):
    nc = bass.Bass("TRN2", target_bir_lowering=False, debug=False)

    d_in = {}
    for name, shape in [
        ("xT", [H, T]), ("L0_attn_w0", [H, FF]), ("L0_attn_w1", [FF, H]),
        ("L0_attn_w2", [H, FF]), ("L0_ffn_clip_x", [1, 1]),
        ("L0_ffn_clip_down", [1, 1]), ("L0_ffn_w_gate", [II, H]),
        ("L0_ffn_w_up", [II, H]), ("L0_ffn_w_down", [H, II]),
        ("wgT", [H, II]), ("wuT", [H, II]), ("wdT", [II, H]),
        ("L1_attn_W_Q", [H, H]), ("L1_attn_W_K", [H, H]),
        ("L1_attn_W_V", [H, H]), ("L1_ffn_weight", [H, H]),
        ("L1_ffn_aux_weight1", [H, H]), ("L1_ffn_aux_weight2", [H, H]),
        ("L1_ffn_bias", [128, 8]),
        ("maskA", [16 * 128, 128]), ("maskB", [16 * 128, 128]),
    ]:
        d_in[name] = nc.dram_tensor(name, shape, F32, kind="ExternalInput").ap()
    outT = nc.dram_tensor("outT", [H, T], F32, kind="ExternalOutput").ap()

    scr_w1 = nc.dram_tensor("scr_w1", [FF, H], BF16).ap()
    scr_wqg = nc.dram_tensor("scr_wqg", [H, II], I8).ap()
    scr_wqu = nc.dram_tensor("scr_wqu", [H, II], I8).ap()
    scr_wqd = nc.dram_tensor("scr_wqd", [II, H], I8).ap()
    scr_xq = nc.dram_tensor("scr_xq", [H, T], I8).ap()
    scr_att0 = nc.dram_tensor("scr_att0", [H, T], F32).ap()
    scr_h = nc.dram_tensor("scr_h", [II, T], F32).ap()
    scr_x1 = nc.dram_tensor("scr_x1", [H, T], F32).ap()
    scr_xn1 = nc.dram_tensor("scr_xn1", [H, T], BF16).ap()
    scr_ffn1 = nc.dram_tensor("scr_ffn1", [H, T], BF16).ap()
    scr_bc = nc.dram_tensor("scr_bc", [1, 8192], F32).ap()
    scr_inv = nc.dram_tensor("scr_inv", [1, 3 * II], F32).ap()
    cc_li = [nc.dram_tensor(f"cc_li{k}", [128, 1], F32).ap() for k in range(NCH)]
    cc_lo = [nc.dram_tensor(f"cc_lo{k}", [128, 1], F32).ap() for k in range(NCH)]
    cc_mxi = nc.dram_tensor("cc_mxi", [128, 1], F32).ap()
    cc_mxo = nc.dram_tensor("cc_mxo", [128, 1], F32).ap()
    cc_mhi = nc.dram_tensor("cc_mhi", [128, 1], F32).ap()
    cc_mho = nc.dram_tensor("cc_mho", [128, 1], F32).ap()
    ag_in = nc.dram_tensor("ag_in", [2 * H * T], BF16).ap()
    ag_out = nc.dram_tensor("ag_out", [2, 2 * H * T], BF16).ap()

    ctx = ExitStack()
    sbt = lambda name, shape, dt: ctx.enter_context(nc.sbuf_tensor(name, shape, dt))

    sb_xn = sbt("sb_xn", [128, NHB * T], BF16)
    sb_big = sbt("sb_big", [128, 65536], BF16)
    sb_tmp = sbt("sb_tmp", [128, 2048], F32)
    sb_qt = sbt("sb_qt", [128, 2048], BF16)
    sb_str = sbt("sb_str", [128, 8192], BF16)
    sb_s32 = sbt("sb_s32", [128, 2816], F32)
    sb_sm = sbt("sb_sm", [128, 1280], F32)
    sb_i8 = sbt("sb_i8", [128, 1024], I8)
    sb_mb = sbt("sb_mb", [128, 512], BF16)
    sb_cst = sbt("sb_cst", [128, 8], F32)
    psb = [ctx.enter_context(nc.psum_tensor(f"psb{i}", [128, 512], F32))
           for i in range(8)]

    SM_D, SM_D2 = 0, 1
    SM_SXI, SM_SX, SM_SHI, SM_SH = 2, 3, 4, 5
    SM_RL, SM_NM, SM_L, SM_TT, SM_CR = 6, 7, 8, 9, 10
    SM_LC = 16
    SM_GMAX = 28
    SM_T = 32
    SM_GS, SM_US, SM_DS = 48, 70, 92
    SM_INV = 100
    SM_MASK = 128
    SM_BCA, SM_BCB = 256, 768

    W0C, W2C = 0, 32768

    def wblk(base, h, j):
        return sb_big[:, base + j * 1024 + h * 128: base + j * 1024 + (h + 1) * 128]

    with ctx:
        mksem = lambda name: Sem(ctx.enter_context(nc.semaphore(name)), name)
        p = Prog(nc)
        te, ve, se, ge = mksem("te"), mksem("ve"), mksem("se"), mksem("ge")
        ccl = mksem("ccl")
        ccm = mksem("ccm")
        dx = [mksem("dx0"), mksem("dx1")]
        dw02 = [mksem("dw02a"), mksem("dw02b")]
        dw1i = [mksem("dw1ia"), mksem("dw1ib")]
        dw1s = [mksem(f"dw1s{i}") for i in range(4)]
        dmisc = mksem("dmisc")
        dnat = [mksem("dnata"), mksem("dnatb")]
        dwt = [mksem("dwta"), mksem("dwtb"), mksem("dwtc"), mksem("dwtd")]
        dqs = [mksem("dqsa"), mksem("dqsb")]
        dbcs = [mksem("dbcs0"), mksem("dbcs1")]
        dbcl = [mksem("dbcl0"), mksem("dbcl1")]
        dstw = mksem("dstw")
        dsta = mksem("dsta")
        dcc = mksem("dcc")
        dwq = mksem("dwq")
        dsth = [mksem("dsth0"), mksem("dsth1")]
        dou = [mksem("dou0"), mksem("dou1")]
        dp3 = [mksem("dp3a"), mksem("dp3b")]
        dkv = [mksem(f"dkv{i}") for i in range(4)]

        def dma(q, sem, out, in_, waits=()):
            p.emit(q, lambda e, o=out, i=in_: e.dma_start(out=o, in_=i),
                   waits=waits, incs=[(sem, 16)])
            return sem.n

        def vop(fn, waits=(), inc=False):
            p.emit("vector", fn, waits=waits, incs=[(ve, 1)] if inc else [])
            p.drain("vector")
            return ve.n

        def sop(fn, waits=(), inc=False):
            p.emit("scalar", fn, waits=waits, incs=[(se, 1)] if inc else [])
            p.drain("scalar")
            return se.n

        def gop(fn, waits=(), inc=False):
            p.emit("gpsimd", fn, waits=waits, incs=[(ge, 1)] if inc else [])
            p.drain("gpsimd")
            return ge.n

        def top(fn, waits=(), inc=False):
            p.emit("tensor", fn, waits=waits, incs=[(te, 1)] if inc else [])
            return te.n

        BC = {"n": 0}

        def bc_slot(n):
            s = BC["n"]
            BC["n"] += n
            assert BC["n"] <= 8192
            return s

        ones_col = sb_mb[:, 0:1]
        zero_col = sb_mb[:, 1:2]
        idn_bf = sb_mb[:, 128:256]
        mask_f = sb_sm[:, SM_MASK:SM_MASK + 128]
        gop(lambda e: e.memset(sb_mb[:, 0:512], 0.0))
        gop(lambda e: e.memset(sb_mb[:, 0:1], 1.0))
        gop(lambda e: e.memset(sb_mb[:, 1:2], 0.0))
        zero_blk = sb_mb[:, 256:384]
        gop(lambda e: e.memset(sb_mb[:, 256:384], 0.0))
        gop(lambda e: e.memset(sb_mb[:, 128:256], 0.0))
        g_con1 = gop(lambda e: e.affine_select(
            out=idn_bf, in_=idn_bf, compare_op=ALU.not_equal, fill=1.0,
            base=0, pattern=[[-1, 128]], channel_multiplier=1), inc=True)
        gop(lambda e: e.memset(mask_f, 0.0))
        g_con2 = gop(lambda e: e.affine_select(
            out=mask_f, in_=mask_f, compare_op=ALU.is_ge, fill=MASK_NEG,
            base=0, pattern=[[-1, 128]], channel_multiplier=1), inc=True)
        v_d0 = vop(lambda e: e.memset(sb_sm[:, SM_D:SM_D + 2], 1.0), inc=True)
        cst_ready = 0
        for ci, cval in enumerate((0.0, 1.0, EPS)):
            cst_ready = gop(lambda e, ci=ci, cval=cval: e.memset(
                sb_cst[:, ci:ci + 1], cval), inc=True)
            nc.const_aps.register(F32, cval, sb_cst[:, ci:ci + 1])

        xn_done = {}
        ln_rows_slot = {}
        v_gmax0 = vop(lambda e: e.memset(sb_sm[:, SM_GMAX:SM_GMAX + 1], 0.0),
                      inc=True)
        ones_f32 = sb_cst[:, 4:5]
        g_of = gop(lambda e: e.memset(sb_cst[:, 4:5], 1.0), inc=True)
        xmm_done = {}
        sq_done = {}
        sqmm_done = {}
        slot_seq = 0
        for tt in range(4):
            for h in range(NHB):
                slot = slot_seq % 2
                slot_seq += 1
                ld = dma("sync", dx[slot], sb_s32[:, slot * 512:(slot + 1) * 512],
                         d_in["xT"][h * 128:(h + 1) * 128, tt * 512:(tt + 1) * 512],
                         waits=[(te, xmm_done.get((tt, h - 2), 0)),
                                (se, sq_done.get((tt - 1, h - 2 + NHB), 0))])
                xmm_done[(tt, h)] = top(lambda e, slot=slot, tt=tt, h=h: e.matmul(
                    psb[2 * tt][0:1, 0:512], ones_f32,
                    sb_s32[:, slot * 512:(slot + 1) * 512],
                    start=(h == 0), stop=(h == NHB - 1)),
                    waits=[(ge, g_of), (dx[slot], ld)], inc=True)
            for h in range(NHB):
                slot = slot_seq % 2
                slot_seq += 1
                ld = dma("sync", dx[slot], sb_s32[:, slot * 512:(slot + 1) * 512],
                         d_in["xT"][h * 128:(h + 1) * 128, tt * 512:(tt + 1) * 512],
                         waits=[(se, sq_done.get((tt, h - 2), 0)),
                                (te, xmm_done.get((tt, h - 2 + NHB), 0))])
                sq_done[(tt, h)] = sop(lambda e, slot=slot: e.activation(
                    out=sb_s32[:, 1024:1536],
                    in_=sb_s32[:, slot * 512:(slot + 1) * 512],
                    func=ACT.Square),
                    waits=[(dx[slot], ld), (ge, cst_ready),
                           (te, sqmm_done.get((tt, h - 1), 0))], inc=True)
                sqmm_done[(tt, h)] = top(lambda e, tt=tt, h=h: e.matmul(
                    psb[2 * tt + 1][0:1, 0:512], ones_f32,
                    sb_s32[:, 1024:1536],
                    start=(h == 0), stop=(h == NHB - 1)),
                    waits=[(se, sq_done[(tt, h)])], inc=True)
        psum_p0_free = {}
        for tt in range(4):
            mm_t = max(xmm_done[(tt, NHB - 1)], sqmm_done[(tt, NHB - 1)])
            ROW_A = sb_sm[0:1, SM_BCA:SM_BCA + 512]
            ROW_B = sb_sm[0:1, SM_BCB:SM_BCB + 512]
            ROW_C = sb_tmp[0:1, 1024:1536]
            vop(lambda e, ROW_A=ROW_A, tt=tt: e.tensor_scalar(
                out=ROW_A, in0=psb[2 * tt][0:1, 0:512],
                scalar1=1.0 / H, scalar2=None, op0=ALU.mult),
                waits=[(te, mm_t)] + ([(ve, xn_done.get((NHB - 1, tt - 1), 0))]
                                      if tt else []))
            psum_p0_free[tt] = vop(lambda e, ROW_B=ROW_B, tt=tt: e.tensor_scalar(
                out=ROW_B, in0=psb[2 * tt + 1][0:1, 0:512],
                scalar1=1.0 / H, scalar2=None, op0=ALU.mult), inc=True)
            vop(lambda e, ROW_A=ROW_A, ROW_C=ROW_C: e.tensor_tensor(
                out=ROW_C, in0=ROW_A, in1=ROW_A, op=ALU.mult))
            v_var = vop(lambda e, ROW_B=ROW_B, ROW_C=ROW_C: e.tensor_tensor(
                out=ROW_C, in0=ROW_B, in1=ROW_C, op=ALU.subtract), inc=True)
            s_sd = sop(lambda e, ROW_B=ROW_B, ROW_C=ROW_C: e.activation(
                out=ROW_B, in_=ROW_C, func=ACT.Sqrt, bias=EPS),
                waits=[(ve, v_var)], inc=True)
            vop(lambda e, ROW_B=ROW_B, ROW_C=ROW_C: e.reciprocal(
                out=ROW_C, in_=ROW_B), waits=[(se, s_sd)])
            vop(lambda e, ROW_A=ROW_A, ROW_C=ROW_C: e.tensor_tensor(
                out=ROW_A, in0=ROW_A, in1=ROW_C, op=ALU.mult))
            v_rows = vop(lambda e, ROW_A=ROW_A: e.tensor_scalar(
                out=ROW_A, in0=ROW_A,
                scalar1=-1.0, scalar2=None, op0=ALU.mult), inc=True)
            bs = bc_slot(1024)
            ln_rows_slot[tt] = bs
            st1 = dma("gpsimd", dbcs[0], scr_bc[0:1, bs:bs + 512],
                      ROW_C, waits=[(ve, v_rows)])
            st2 = dma("gpsimd", dbcs[1], scr_bc[0:1, bs + 512:bs + 1024],
                      ROW_A)
            ld1 = dma("gpsimd", dbcl[0], sb_sm[:, SM_BCA:SM_BCA + 512],
                      scr_bc[0:1, bs:bs + 512].broadcast_to([128, 512]),
                      waits=[(dbcs[0], st1), (dbcs[1], st2)])
            ld2 = dma("gpsimd", dbcl[1], sb_sm[:, SM_BCB:SM_BCB + 512],
                      scr_bc[0:1, bs + 512:bs + 1024].broadcast_to([128, 512]),
                      waits=[])
            for h in range(NHB):
                i = tt * NHB + h
                slot = i % 2
                xt_ = sb_xn[:, h * T + tt * 512:h * T + (tt + 1) * 512]
                t1 = sb_s32[:, 1024:1536]
                ia = tt * NHB + h
                ld = dma("sync", dx[slot], sb_s32[:, slot * 512:(slot + 1) * 512],
                         d_in["xT"][h * 128:(h + 1) * 128, tt * 512:(tt + 1) * 512],
                         waits=[(ve, xn_done.get(((ia - 2) % NHB,
                                                  (ia - 2) // NHB), 0)
                                 if ia >= 2 else 0),
                                (te, sqmm_done[(3, NHB - 1)]) if (tt == 0 and
                                h < 2) else (ve, 0)])
                vop(lambda e, slot=slot, t1=t1: e.tensor_tensor(
                    out=t1, in0=sb_s32[:, slot * 512:(slot + 1) * 512],
                    in1=sb_sm[:, SM_BCA:SM_BCA + 512], op=ALU.mult),
                    waits=[(dx[slot], ld)] +
                          ([(dbcl[0], ld1), (dbcl[1], ld2)] if h == 0 else []))
                vop(lambda e, t1=t1: e.tensor_tensor(
                    out=t1, in0=t1, in1=sb_sm[:, SM_BCB:SM_BCB + 512],
                    op=ALU.add))
                vop(lambda e, t1=t1: e.tensor_reduce(
                    out=sb_sm[:, SM_GMAX + 1:SM_GMAX + 2], in_=t1, axis=AX,
                    op=ALU.max, apply_absolute_value=True))
                vop(lambda e: e.tensor_tensor(
                    out=sb_sm[:, SM_GMAX:SM_GMAX + 1],
                    in0=sb_sm[:, SM_GMAX:SM_GMAX + 1],
                    in1=sb_sm[:, SM_GMAX + 1:SM_GMAX + 2], op=ALU.max),
                    waits=[(ve, v_gmax0)] if (h == 0 and tt == 0) else [])
                xn_done[(h, tt)] = vop(lambda e, xt_=xt_, t1=t1: e.tensor_copy(
                    out=xt_, in_=t1), inc=True)
            p.waits("gpsimd", [(ve, xn_done[(NHB - 1, tt)])])
        xn_tile = {tt: xn_done[(NHB - 1, tt)] for tt in range(4)}
        xn_all = xn_tile[3]

        v_mx = vop(lambda e: e.tensor_copy(
            out=sb_sm[:, SM_GMAX + 1:SM_GMAX + 2],
            in_=sb_sm[:, SM_GMAX:SM_GMAX + 1]),
            waits=[(ve, xn_all)], inc=True)
        stm = dma("gpsimd", dcc, cc_mxi[:, :], sb_sm[:, SM_GMAX + 1:SM_GMAX + 2],
                  waits=[(ve, v_mx)])
        p.emit("gpsimd", lambda e: e.collective_compute(
            "AllReduce", ALU.max, replica_groups=[list(range(8))],
            ins=[cc_mxi], outs=[cc_mxo]), waits=[(dcc, stm)], incs=[(ccm, 1)])
        ar_mx = ccm.n
        ldm = dma("gpsimd", dbcl[0], sb_sm[:, SM_BCA:SM_BCA + 128],
                  cc_mxo.flatten().unsqueeze(0).broadcast_to([128, 128]),
                  waits=[(ccm, ar_mx), (ve, xn_all)])
        v_gmax = vop(lambda e: e.tensor_reduce(
            out=sb_sm[:, SM_GMAX:SM_GMAX + 1], in_=sb_sm[:, SM_BCA:SM_BCA + 128],
            axis=AX, op=ALU.max), waits=[(dbcl[0], ldm)], inc=True)

        dma("sync", dmisc, sb_sm[:, SM_T:SM_T + 1],
            d_in["L0_ffn_clip_x"].broadcast_to([128, 1]))
        lcx = dma("sync", dmisc, sb_sm[:, SM_T + 1:SM_T + 2],
                  d_in["L0_ffn_clip_down"].broadcast_to([128, 1]))
        s_sig = sop(lambda e: e.activation(
            out=sb_sm[:, SM_T + 2:SM_T + 4], in_=sb_sm[:, SM_T:SM_T + 2],
            func=ACT.Sigmoid), waits=[(dmisc, lcx)], inc=True)
        vop(lambda e: e.tensor_tensor(
            out=sb_sm[:, SM_SX:SM_SX + 1], in0=sb_sm[:, SM_T + 2:SM_T + 3],
            in1=sb_sm[:, SM_GMAX:SM_GMAX + 1], op=ALU.mult),
            waits=[(se, s_sig), (ve, v_gmax)])
        vop(lambda e: e.tensor_scalar(
            out=sb_sm[:, SM_SX:SM_SX + 1], in0=sb_sm[:, SM_SX:SM_SX + 1],
            scalar1=1.0 / 127.0, scalar2=None, op0=ALU.mult))
        v_sx = vop(lambda e: e.reciprocal(
            out=sb_sm[:, SM_SXI:SM_SXI + 1], in_=sb_sm[:, SM_SX:SM_SX + 1]),
            inc=True)

        wg_done, wu_done = {}, {}
        wcast = {}
        for j in range(NFB):
            for t, (src, base) in enumerate(
                    ((d_in["L0_attn_w0"], W0C), (d_in["L0_attn_w2"], W2C))):
                idx = j * 2 + t
                stage = sb_s32[:, 1536:2560] if idx % 2 == 0 else sb_tmp[:, 0:1024]
                ld = dma("sync", dw02[idx % 2],
                         stage.rearrange("p (h c) -> p h c", h=8),
                         src[:, j * 128:(j + 1) * 128].rearrange(
                             "(h p) c -> p h c", p=128),
                         waits=[(ve, wcast.get(idx - 2, 0))])
                wcast[idx] = vop(lambda e, j=j, base=base, stage=stage:
                                 e.tensor_copy(
                                     out=sb_big[:, base + j * 1024:base + (j + 1) * 1024],
                                     in_=stage),
                                 waits=[(dw02[idx % 2], ld)], inc=True)
                if t == 0:
                    wg_done[j] = wcast[idx]
                else:
                    wu_done[j] = wcast[idx]
        w02_all = wcast[2 * NFB - 1]

        w1cast = {}
        w1_ready = {}
        for j in range(NFB):
            stage = sb_tmp[:, 0:1024] if j % 2 == 0 else sb_s32[:, 1536:2560]
            w = [(ve, w02_all)] if j < 2 else [(ve, w1cast[j - 2])]
            ld = dma("sync", dw1i[j % 2], stage,
                     d_in["L0_attn_w1"][j * 128:(j + 1) * 128, :], waits=w)
            w1cast[j] = vop(lambda e, stage=stage, j=j: e.tensor_copy(
                out=sb_str[:, (j % 2) * 1024:(j % 2 + 1) * 1024], in_=stage),
                waits=[(dw1i[j % 2], ld), (dstw, w1_ready.get(j - 1, 0))],
                inc=True)
            w1_ready[j] = dma("gpsimd", dstw, scr_w1[j * 128:(j + 1) * 128, :],
                              sb_str[:, (j % 2) * 1024:(j % 2 + 1) * 1024],
                              waits=[(ve, w1cast[j])])
        w1cast_all = w1cast[NFB - 1]

        if upto == 0:
            done = []
            for h in range(NHB):
                for half in range(2):
                    cv = vop(lambda e, h=h, half=half: e.tensor_copy(
                        out=sb_s32[:, 0:1024],
                        in_=sb_xn[:, h * T + half * 1024:h * T + (half + 1) * 1024]),
                        waits=[(ve, xn_all)] + ([(dou, done[-1])] if done else []),
                        inc=True)
                    done.append(dma("gpsimd", dou,
                                    outT[h * 128:(h + 1) * 128,
                                         half * 1024:(half + 1) * 1024],
                                    sb_s32[:, 0:1024], waits=[(ve, cv)]))
            p.waits("gpsimd", [(dou, done[-1])])
            _close(p, nc)
            return nc

        QSTG = 1536
        QINV = 512

        def emit_quant_scales():
            specs = [("g", d_in["L0_ffn_w_gate"], NIB, SM_GS, 0),
                     ("u", d_in["L0_ffn_w_up"], NIB, SM_US, II),
                     ("d", d_in["L0_ffn_w_down"], NHB, SM_DS, 2 * II)]
            prev_red = 0
            for tag, nat, nb, smc, invbase in specs:
                ncols = nat.shape[1]
                for ib in range(nb):
                    nsub = (ncols + 1023) // 1024
                    for sx in range(nsub):
                        c0, c1 = sx * 1024, min((sx + 1) * 1024, ncols)
                        ld = dma("scalar", dnat[0],
                                 sb_s32[:, QSTG:QSTG + (c1 - c0)],
                                 nat[ib * 128:(ib + 1) * 128, c0:c1],
                                 waits=[(ve, prev_red)])
                        prev_red = vop(lambda e, w_=(c1 - c0): e.tensor_reduce(
                            out=sb_sm[:, SM_T + 8:SM_T + 9],
                            in_=sb_s32[:, QSTG:QSTG + w_], axis=AX,
                            op=ALU.max, apply_absolute_value=True),
                            waits=[(dnat[0], ld)], inc=True)
                        if sx == 0:
                            vop(lambda e, smc=smc, ib=ib: e.tensor_copy(
                                out=sb_sm[:, smc + ib:smc + ib + 1],
                                in_=sb_sm[:, SM_T + 8:SM_T + 9]))
                        else:
                            vop(lambda e, smc=smc, ib=ib: e.tensor_tensor(
                                out=sb_sm[:, smc + ib:smc + ib + 1],
                                in0=sb_sm[:, smc + ib:smc + ib + 1],
                                in1=sb_sm[:, SM_T + 8:SM_T + 9], op=ALU.max))
                    vop(lambda e, smc=smc, ib=ib: e.reciprocal(
                        out=sb_sm[:, SM_T + 9:SM_T + 10],
                        in_=sb_sm[:, smc + ib:smc + ib + 1]),
                        waits=[(dwq, dwq.n)])
                    v_inv = vop(lambda e: e.tensor_scalar(
                        out=sb_sm[:, SM_T + 9:SM_T + 10],
                        in0=sb_sm[:, SM_T + 9:SM_T + 10],
                        scalar1=127.0, scalar2=None, op0=ALU.mult), inc=True)
                    dma("gpsimd", dwq,
                        scr_inv[0:1, invbase + ib * 128:invbase + (ib + 1) * 128]
                        .rearrange("one c -> c one"),
                        sb_sm[:, SM_T + 9:SM_T + 10],
                        waits=[(ve, v_inv)])
                    vop(lambda e, smc=smc, ib=ib: e.tensor_scalar(
                        out=sb_sm[:, smc + ib:smc + ib + 1],
                        in0=sb_sm[:, smc + ib:smc + ib + 1],
                        scalar1=1.0 / 127.0, scalar2=None, op0=ALU.mult))
                    yield

        def emit_quant_wT():
            specs = [(d_in["wgT"], scr_wqg, NHB, II, 0),
                     (d_in["wuT"], scr_wqu, NHB, II, II),
                     (d_in["wdT"], scr_wqd, NIB, H, 2 * II)]
            for wt, scr, nb, ncols, invbase in specs:
                crs = [(c, min(c + 1024, ncols)) for c in range(0, ncols, 1024)]
                for (c0, c1) in crs:
                    wdt = c1 - c0
                    ldi = dma("scalar", dwt[0], sb_s32[:, QINV:QINV + wdt],
                              scr_inv[0:1, invbase + c0:invbase + c1]
                              .broadcast_to([128, wdt]),
                              waits=[(dwq, dwq.n), (ve, ve.n)])
                    prev_q = 0
                    for rb in range(nb):
                        ldw_ = dma("scalar", dwt[1],
                                   sb_s32[:, QSTG:QSTG + wdt],
                                   wt[rb * 128:(rb + 1) * 128, c0:c1],
                                   waits=[(ve, prev_q if rb else ve.n)])
                        vop(lambda e, wdt=wdt: e.tensor_tensor(
                            out=sb_s32[:, QSTG:QSTG + wdt],
                            in0=sb_s32[:, QSTG:QSTG + wdt],
                            in1=sb_s32[:, QINV:QINV + wdt], op=ALU.mult),
                            waits=[(dwt[1], ldw_), (dwt[0], ldi),
                                   (dwq, dwq.n)])
        
                        prev_q = v_q = vop(lambda e, wdt=wdt: e.tensor_scalar(
                            out=sb_i8[:, 0:wdt],
                            in0=sb_s32[:, QSTG:QSTG + wdt],
                            scalar1=127.49, scalar2=-128.49,
                            op0=ALU.min, op1=ALU.max), inc=True)
                        dma("gpsimd", dwq, scr[rb * 128:(rb + 1) * 128, c0:c1],
                            sb_i8[:, 0:wdt], waits=[(ve, v_q)])
                        yield

        def emit_quant_xq():
            prev_q = 0
            for tt in range(4):
                bs = ln_rows_slot[tt]
                ldr = dma("scalar", dwt[0], sb_s32[:, 512:1024],
                          scr_bc[0:1, bs:bs + 512].broadcast_to([128, 512]),
                          waits=[(ve, ve.n), (dwq, dwq.n)])
                ldn = dma("scalar", dwt[1], sb_s32[:, 1024:1536],
                          scr_bc[0:1, bs + 512:bs + 1024]
                          .broadcast_to([128, 512]),
                          waits=[(ve, ve.n)])
                for h in range(NHB):
                    ldx2 = dma("scalar", dnat[1],
                               sb_s32[:, QSTG:QSTG + 512],
                               d_in["xT"][h * 128:(h + 1) * 128,
                                          tt * 512:(tt + 1) * 512],
                               waits=[(ve, prev_q)])
                    vop(lambda e: e.tensor_tensor(
                        out=sb_s32[:, QSTG:QSTG + 512],
                        in0=sb_s32[:, QSTG:QSTG + 512],
                        in1=sb_s32[:, 512:1024], op=ALU.mult),
                        waits=[(dnat[1], ldx2), (dwt[0], ldr), (dwt[1], ldn),
                               (ve, v_sx), (dwq, dwq.n)])
                    vop(lambda e: e.tensor_tensor(
                        out=sb_s32[:, QSTG:QSTG + 512],
                        in0=sb_s32[:, QSTG:QSTG + 512],
                        in1=sb_s32[:, 1024:1536], op=ALU.add))
                    vop(lambda e: e.tensor_scalar(
                        out=sb_s32[:, QSTG:QSTG + 512],
                        in0=sb_s32[:, QSTG:QSTG + 512],
                        scalar1=sb_sm[:, SM_SXI:SM_SXI + 1], scalar2=127.49,
                        op0=ALU.mult, op1=ALU.min))
                    prev_q = v_q = vop(lambda e: e.tensor_scalar(
                        out=sb_i8[:, 0:512], in0=sb_s32[:, QSTG:QSTG + 512],
                        scalar1=-128.49, scalar2=None, op0=ALU.max), inc=True)
                    dma("gpsimd", dwq,
                        scr_xq[h * 128:(h + 1) * 128, tt * 512:(tt + 1) * 512],
                        sb_i8[:, 0:512], waits=[(ve, v_q)])
                    yield

        BGQ = {"gens": None, "done": False}

        def pump_bgq(n):
            if BGQ["gens"] is None:
                import itertools
                BGQ["gens"] = itertools.chain(
                    emit_quant_scales(), emit_quant_wT(), emit_quant_xq())
            g = BGQ["gens"]
            for _ in range(n):
                try:
                    next(g)
                except StopIteration:
                    BGQ["done"] = True
                    return

        LAG = 12
        SLOT = [None] * 24

        def mkslot(i):
            if i < 8:
                return sb_qt[:, i * 256:(i + 1) * 256]
            if i < 16:
                return sb_str[:, (i - 8) * 256:(i - 7) * 256]
            return sb_str[:, 6144 + (i - 16) * 256:6144 + (i - 15) * 256]
        for i in range(24):
            SLOT[i] = mkslot(i)

        def g_sb(gj):
            return SLOT[gj % LAG]

        def u_sb(gj):
            return SLOT[12 + gj % LAG]

        g_done, u_done, silu_done, h_ready = {}, {}, {}, {}
        gev, uev = {}, {}
        out_g = {}
        evac_done, store_done = {}, {}
        d_ready = {0: v_d0}
        w1_ld = {}

        def w1slab(slot):
            return sb_str[:, 2048 + slot * 1024: 2048 + (slot + 1) * 1024]

        def xc(m, k):
            return sb_xn[:, m * T + k * CHT:m * T + (k + 1) * CHT]

        LDPTR = {"n": 0}

        def pump_w1(upto_g):
            while LDPTR["n"] <= min(upto_g, NCH * NFB - 1):
                g = LDPTR["n"]
                w = [(te, out_g[g - 4])] if g >= 4 else \
                    [(dstw, w1_ready[NFB - 1])]
                w1_ld[g] = dma("sync", dw1s[g % 4], w1slab(g % 4),
                               scr_w1[(g % NFB) * 128:(g % NFB + 1) * 128, :],
                               waits=w)
                LDPTR["n"] += 1

        def emit_gu(k, j):
            gj = k * NFB + j
            tt = k // 2
            gb, ub = j % 2, 2 + j % 2
            w = [(ve, gev.get(gj - 2, 0))]
            if k == 0:
                w.append((ve, wg_done[j]))
                if j == 0:
                    w.append((ve, psum_p0_free[1]))
            if j == 0:
                w.append((ve, xn_tile[tt]))
            for m in range(NHB):
                g_done[gj] = top(lambda e, k=k, j=j, m=m, gb=gb: e.matmul(
                    psb[gb][:, 0:256], wblk(W0C, m, j), xc(m, k),
                    start=(m == 0), stop=(m == NHB - 1)),
                    waits=w if m == 0 else [], inc=(m == NHB - 1))
            w = [(ve, uev.get(gj - 2, 0))]
            if k == 0:
                w.append((ve, wu_done[j]))
                if j == 0:
                    w.append((ve, psum_p0_free[3]))
            for m in range(NHB):
                u_done[gj] = top(lambda e, k=k, j=j, m=m, ub=ub: e.matmul(
                    psb[ub][:, 0:256], wblk(W2C, m, j), xc(m, k),
                    start=(m == 0), stop=(m == NHB - 1)),
                    waits=w if m == 0 else [], inc=(m == NHB - 1))

        def emit_guev(k, j):
            gj = k * NFB + j
            gb, ub = j % 2, 2 + j % 2
            w = [(te, g_done[gj])]
            if gj >= LAG:
                w.append((ve, h_ready.get(gj - LAG, 0)))
            if gj < 2 * NFB:
                w.append((dstw, w1_ready.get(NFB - 1, 0)))
            gev[gj] = vop(lambda e, gj=gj, gb=gb: e.tensor_copy(
                out=g_sb(gj), in_=psb[gb][:, 0:256]), waits=w, inc=True)
            w = [(te, u_done[gj])]
            if gj >= LAG:
                w.append((te, out_g.get(gj - LAG, 0)))
            uev[gj] = vop(lambda e, gj=gj, ub=ub: e.tensor_copy(
                out=u_sb(gj), in_=psb[ub][:, 0:256]), waits=w, inc=True)

        def emit_silu_h(k, j):
            gj = k * NFB + j
            silu_done[gj] = sop(lambda e, gj=gj: e.activation(
                out=g_sb(gj), in_=g_sb(gj), func=ACT.Silu,
                scale=sb_sm[:, SM_D:SM_D + 1]),
                waits=[(ve, gev[gj]), (ve, d_ready[k])], inc=True)
            h_ready[gj] = vop(lambda e, gj=gj: e.tensor_tensor(
                out=u_sb(gj), in0=g_sb(gj), in1=u_sb(gj), op=ALU.mult),
                waits=[(se, silu_done[gj]), (ve, uev[gj])], inc=True)

        def emit_out(k, j, clear_t):
            gj = k * NFB + j
            pump_w1(gj + 3)
            w = [(ve, h_ready[gj]), (dw1s[gj % 4], w1_ld[gj]), (te, clear_t),
                 (ve, evac_done.get(k - 1, 0))]
            for m in range(NHB):
                out_g[gj] = top(lambda e, m=m, j=j, gj=gj: e.matmul(
                    psb[4 + m // 2][:, (m % 2) * 256:(m % 2 + 1) * 256],
                    w1slab(gj % 4)[:, m * 128:(m + 1) * 128],
                    u_sb(gj), start=False, stop=(j == NFB - 1),
                    skip_group_check=True),
                    waits=w if m == 0 else [], inc=(m == NHB - 1))
                w = []

        def emit_evac_loss(k):
            od = out_g[k * NFB + NFB - 1]
            for m in range(NHB):
                w = [(te, od)]
                if m == 0:
                    w.append((dsta, store_done.get(k - 1, 0)))
                    if k == 0:
                        w.append((ve, w1cast_all))
                vop(lambda e, m=m: e.tensor_scalar(
                    out=sb_tmp[:, m * 256:(m + 1) * 256],
                    in0=psb[4 + m // 2][:, (m % 2) * 256:(m % 2 + 1) * 256],
                    scalar1=sb_sm[:, SM_D2:SM_D2 + 1], scalar2=None,
                    op0=ALU.mult), waits=w)
                vop(lambda e, m=m, k=k: e.tensor_tensor(
                    out=sb_s32[:, 0:256], in0=sb_tmp[:, m * 256:(m + 1) * 256],
                    in1=xc(m, k), op=ALU.subtract))
                vop(lambda e, m=m: e.tensor_tensor(
                    out=sb_s32[:, 256:512], in0=sb_s32[:, 0:256],
                    in1=sb_s32[:, 0:256], op=ALU.mult))
                vop(lambda e, m=m: e.reduce_sum(
                    out=sb_sm[:, SM_LC + m:SM_LC + m + 1],
                    in_=sb_s32[:, 256:512], axis=AX))
            evac_done[k] = vop(lambda e: e.reduce_sum(
                out=sb_sm[:, SM_LC + 8:SM_LC + 9],
                in_=sb_sm[:, SM_LC:SM_LC + 8], axis=AX), inc=True)
            tgt = outT if (dbg == "att0") else scr_att0
            store_done[k] = dma(
                "gpsimd", dsta,
                tgt[:, k * CHT:(k + 1) * CHT].rearrange("(h p) c -> p h c", p=128),
                sb_tmp[:, 0:2048].rearrange("p (h c) -> p h c", h=8),
                waits=[(ve, evac_done[k])])
            stt_ = dma("gpsimd", dcc, cc_li[k][:, :],
                       sb_sm[:, SM_LC + 8:SM_LC + 9],
                       waits=[(ve, evac_done[k])])
            p.emit("gpsimd", lambda e, k=k: e.collective_compute(
                "AllReduce", ALU.add, replica_groups=[list(range(8))],
                ins=[cc_li[k]], outs=[cc_lo[k]]),
                waits=[(dcc, stt_)], incs=[(ccl, 1)])
            return ccl.n

        def emit_dupdate(k, ar_k):
            ldb = dma("gpsimd", dbcl[1], sb_sm[:, SM_BCB:SM_BCB + 128],
                      cc_lo[k].flatten().unsqueeze(0).broadcast_to([128, 128]),
                      waits=[(ccl, ar_k)])
            vop(lambda e: e.tensor_reduce(
                out=sb_sm[:, SM_LC + 9:SM_LC + 10],
                in_=sb_sm[:, SM_BCB:SM_BCB + 128], axis=AX, op=ALU.add),
                waits=[(dbcl[1], ldb)])
            vop(lambda e: e.tensor_scalar(
                out=sb_sm[:, SM_LC + 9:SM_LC + 10],
                in0=sb_sm[:, SM_LC + 9:SM_LC + 10],
                scalar1=-LR * LOSS_SCALE, scalar2=1.0, op0=ALU.mult,
                op1=ALU.add))
            vop(lambda e: e.tensor_tensor(
                out=sb_sm[:, SM_D:SM_D + 1], in0=sb_sm[:, SM_D:SM_D + 1],
                in1=sb_sm[:, SM_LC + 9:SM_LC + 10], op=ALU.mult))
            d_ready[k + 1] = vop(lambda e: e.tensor_tensor(
                out=sb_sm[:, SM_D2:SM_D2 + 1], in0=sb_sm[:, SM_D:SM_D + 1],
                in1=sb_sm[:, SM_D:SM_D + 1], op=ALU.mult), inc=True)

        for k in range(NCH):
            clear_t = None
            for j in range(NFB):
                if j == LAG:
                    for bk in range(4, 8):
                        clear_t = top(lambda e, bk=bk: e.matmul(
                            psb[bk][:, 0:512], zero_blk, sb_mb[:, 0:512],
                            start=True, stop=True, skip_group_check=True),
                            waits=(([(ge, g_con1),
                                     (ve, evac_done.get(k - 1, 0))] +
                                    ([(ve, psum_p0_free[3])] if k == 0
                                     else [])) if bk == 4 else []),
                            inc=(bk == 7))
                if j >= LAG:
                    emit_out(k, j - LAG, clear_t)
                emit_gu(k, j)
                emit_guev(k, j)
                if k > 0 and j == 5:
                    emit_dupdate(k - 1, ar_prev_saved)
                if j >= 6:
                    emit_silu_h(k, j - 6)
            for j in range(NFB - 6, NFB):
                emit_silu_h(k, j)
            for j in range(NFB - LAG, NFB):
                emit_out(k, j, clear_t)
            ar_prev_saved = emit_evac_loss(k)
            if k >= 1:
                pump_bgq(40)
        emit_dupdate(NCH - 1, ar_prev_saved)
        while not BGQ["done"]:
            pump_bgq(40)
        p1_ve = evac_done[NCH - 1]
        p1_st = store_done[NCH - 1]

        if upto == 1:
            p.waits("gpsimd", [(dsta, p1_st), (ve, p1_ve)])
            _close(p, nc)
            return nc

        evac2_done = {}
        WQUC, XQRC = 22528, 45056
        big8 = sb_big.bitcast(I8)

        def wqgblk(wi, hb, ib):
            off = wi * WQUC + hb * 2816 + ib * 128
            return sb_big[:, off:off + 128]

        def xqr(tt, hb):
            off = XQRC + tt * 4096 + hb * 512
            return sb_big[:, off:off + 512]

        v_comb = 0
        for smc in (SM_GS, SM_US):
            for ib in range(NIB):
                v_comb = vop(lambda e, smc=smc, ib=ib: e.tensor_scalar(
                    out=sb_sm[:, smc + ib:smc + ib + 1],
                    in0=sb_sm[:, smc + ib:smc + ib + 1],
                    scalar1=sb_sm[:, SM_SX:SM_SX + 1], scalar2=None,
                    op0=ALU.mult),
                    waits=[(ve, v_sx), (ve, p1_ve)] if (smc == SM_GS and ib == 0)
                    else [], inc=True)

        p1_te = te.n
        prev_cv = 0
        xq_cv = {}
        for tt in range(4):
            for hb in range(NHB):
                w = [(dwq, dwq.n), (se, prev_cv)]
                if tt == 0 and hb == 0:
                    w += [(te, p1_te), (ve, p1_ve), (dsta, p1_st)]
                ld = dma("sync", dqs[0], sb_i8[:, 0:512],
                         scr_xq[hb * 128:(hb + 1) * 128, tt * 512:(tt + 1) * 512],
                         waits=w)
                prev_cv = sop(lambda e, tt=tt, hb=hb: e.activation(
                    out=xqr(tt, hb), in_=sb_i8[:, 0:512], func=ACT.Copy),
                    waits=[(dqs[0], ld)], inc=True)
            xq_cv[tt] = prev_cv
        wq_cv = {}
        for wi, scr in enumerate((scr_wqg, scr_wqu)):
            for hb in range(NHB):
                for c0 in (0, 1024, 2048):
                    wdt = min(1024, II - c0)
                    ld = dma("sync", dqs[0], sb_i8[:, 0:wdt],
                             scr[hb * 128:(hb + 1) * 128, c0:c0 + wdt],
                             waits=[(dwq, dwq.n), (se, prev_cv)])
                    prev_cv = sop(lambda e, wi=wi, hb=hb, c0=c0, wdt=wdt:
                                  e.activation(
                                      out=sb_big[:, wi * WQUC + hb * 2816 + c0:
                                                 wi * WQUC + hb * 2816 + c0 + wdt],
                                      in_=sb_i8[:, 0:wdt], func=ACT.Copy),
                                  waits=[(dqs[0], ld)], inc=True)
            wq_cv[wi] = prev_cv

        silu1_done, h_done, gate_done, up_done = {}, {}, {}, {}
        hstore = {}
        seq = []
        hmax_init = vop(lambda e: e.memset(sb_sm[:, SM_T + 10:SM_T + 11], 0.0),
                        inc=True)
        for tt in range(4):
            for ib in range(NIB):
                prev = seq[-2] if len(seq) >= 2 else None
                seq.append((ib, tt))
                n = len(seq)
                gb, ub = n % 2, 2 + n % 2
                w = [(se, wq_cv[0]), (se, xq_cv[tt])]
                if prev is not None:
                    w.append((se, silu1_done[prev]))
                for hb in range(NHB):
                    gate_done[(ib, tt)] = top(
                        lambda e, ib=ib, tt=tt, hb=hb, gb=gb: e.matmul(
                            psb[gb][:, 0:512], wqgblk(0, hb, ib),
                            xqr(tt, hb), start=(hb == 0),
                            stop=(hb == NHB - 1)),
                        waits=w if hb == 0 else [], inc=(hb == NHB - 1))
                w = [(se, wq_cv[1])]
                if prev is not None:
                    w.append((ve, h_done[prev]))
                for hb in range(NHB):
                    up_done[(ib, tt)] = top(
                        lambda e, ib=ib, tt=tt, hb=hb, ub=ub: e.matmul(
                            psb[ub][:, 0:512], wqgblk(1, hb, ib),
                            xqr(tt, hb), start=(hb == 0),
                            stop=(hb == NHB - 1)),
                        waits=w if hb == 0 else [], inc=(hb == NHB - 1))
                sl = sb_s32[:, 1536 + (n % 2) * 512:2048 + (n % 2) * 512]
                silu1_done[(ib, tt)] = sop(
                    lambda e, ib=ib, gb=gb, sl=sl: e.activation(
                        out=sl, in_=psb[gb][:, 0:512], func=ACT.Silu,
                        scale=sb_sm[:, SM_GS + ib:SM_GS + ib + 1]),
                    waits=[(te, gate_done[(ib, tt)]), (ve, v_comb)] +
                          ([(ve, h_done[prev])] if prev is not None else []),
                    inc=True)
                hs = sb_tmp[:, (n % 2) * 512:(n % 2 + 1) * 512]
                vop(lambda e, ib=ib, ub=ub, sl=sl, hs=hs:
                    e.scalar_tensor_tensor(
                        out=hs, in0=psb[ub][:, 0:512],
                        scalar=sb_sm[:, SM_US + ib:SM_US + ib + 1],
                        in1=sl, op0=ALU.mult, op1=ALU.mult),
                    waits=[(se, silu1_done[(ib, tt)]),
                           (te, up_done[(ib, tt)]),
                           (dsth[n % 2],
                            hstore.get(seq[-3] if n >= 3 else None, 0))])
                vop(lambda e, hs=hs: e.tensor_reduce(
                    out=sb_sm[:, SM_T + 11:SM_T + 12],
                    in_=hs, axis=AX, op=ALU.max,
                    apply_absolute_value=True))
                h_done[(ib, tt)] = vop(lambda e: e.tensor_tensor(
                    out=sb_sm[:, SM_T + 10:SM_T + 11],
                    in0=sb_sm[:, SM_T + 10:SM_T + 11],
                    in1=sb_sm[:, SM_T + 11:SM_T + 12], op=ALU.max),
                    inc=True)
                hstore[(ib, tt)] = dma(
                    "gpsimd", dsth[n % 2],
                    scr_h[ib * 128:(ib + 1) * 128, tt * 512:(tt + 1) * 512],
                    hs, waits=[(ve, h_done[(ib, tt)])])
        sweep1_done = h_done[seq[-1]]
        sweep1_te = up_done[seq[-1]]
        sweep1_st0 = hstore[seq[-1]]
        sweep1_st1 = hstore[seq[-2]]

        if dbg == "h":
            p.waits("gpsimd", [(dsth[0], sweep1_st0), (dsth[1], sweep1_st1)])
            _close(p, nc)
            return nc

        stm2 = dma("gpsimd", dcc, cc_mhi[:, :], sb_sm[:, SM_T + 10:SM_T + 11],
                   waits=[(ve, sweep1_done)])
        p.emit("gpsimd", lambda e: e.collective_compute(
            "AllReduce", ALU.max, replica_groups=[list(range(8))],
            ins=[cc_mhi], outs=[cc_mho]), waits=[(dcc, stm2)], incs=[(ccm, 1)])
        ar_mh = ccm.n
        ldm2 = dma("gpsimd", dbcl[0], sb_sm[:, SM_BCA:SM_BCA + 128],
                   cc_mho.flatten().unsqueeze(0).broadcast_to([128, 128]),
                   waits=[(ccm, ar_mh)])
        vop(lambda e: e.tensor_reduce(
            out=sb_sm[:, SM_SH:SM_SH + 1], in_=sb_sm[:, SM_BCA:SM_BCA + 128],
            axis=AX, op=ALU.max), waits=[(dbcl[0], ldm2)])
        vop(lambda e: e.tensor_tensor(
            out=sb_sm[:, SM_SH:SM_SH + 1], in0=sb_sm[:, SM_T + 3:SM_T + 4],
            in1=sb_sm[:, SM_SH:SM_SH + 1], op=ALU.mult))
        vop(lambda e: e.tensor_scalar(
            out=sb_sm[:, SM_SH:SM_SH + 1], in0=sb_sm[:, SM_SH:SM_SH + 1],
            scalar1=1.0 / 127.0, scalar2=None, op0=ALU.mult))
        v_sh = vop(lambda e: e.reciprocal(
            out=sb_sm[:, SM_SHI:SM_SHI + 1], in_=sb_sm[:, SM_SH:SM_SH + 1]),
            inc=True)
        v_combd = 0
        for hb in range(NHB):
            v_combd = vop(lambda e, hb=hb: e.tensor_scalar(
                out=sb_sm[:, SM_DS + hb:SM_DS + hb + 1],
                in0=sb_sm[:, SM_DS + hb:SM_DS + hb + 1],
                scalar1=sb_sm[:, SM_SH:SM_SH + 1], scalar2=None,
                op0=ALU.mult), waits=[(ve, v_sh)] if hb == 0 else [],
                inc=True)

        hq_done = {}
        wqd_cv = {}
        prev_hld = 0
        prev_wld = 0
        for ib in range(NIB):
            for half in range(2):
                ldh = dma("sync", dqs[0], sb_s32[:, 1536:2560],
                          scr_h[ib * 128:(ib + 1) * 128,
                                half * 1024:(half + 1) * 1024],
                          waits=[(ve, prev_hld)] +
                                ([(dsth[0], sweep1_st0),
                                  (dsth[1], sweep1_st1), (te, sweep1_te)]
                                 if (ib == 0 and half == 0) else []))
                vop(lambda e: e.tensor_scalar(
                    out=sb_s32[:, 1536:2560], in0=sb_s32[:, 1536:2560],
                    scalar1=sb_sm[:, SM_SHI:SM_SHI + 1], scalar2=127.49,
                    op0=ALU.mult, op1=ALU.min),
                    waits=[(dqs[0], ldh), (ve, v_sh)])
                v_q = vop(lambda e: e.tensor_scalar(
                    out=sb_i8[:, 0:1024], in0=sb_s32[:, 1536:2560],
                    scalar1=-128.49, scalar2=None, op0=ALU.max), inc=True)
                prev_hld = hq_done[(ib, half)] = vop(
                    lambda e, ib=ib, half=half: e.tensor_copy(
                        out=big8[:, ib * 2048 + half * 1024:
                                 ib * 2048 + (half + 1) * 1024],
                        in_=sb_i8[:, 0:1024]), inc=True)
            wst = big8[:, 126976 + (ib % 2) * 1024:126976 + (ib % 2 + 1) * 1024]
            ldw2 = dma("sync", dqs[1], wst,
                       scr_wqd[ib * 128:(ib + 1) * 128, :],
                       waits=[(se, prev_wld), (te, sweep1_te)
                              if ib < 2 else (ve, 0)])
            prev_wld = wqd_cv[ib] = sop(lambda e, ib=ib, wst=wst: e.activation(
                out=sb_big[:, WQUC + ib * 1024:WQUC + (ib + 1) * 1024],
                in_=wst, func=ACT.Copy),
                waits=[(dqs[1], ldw2)], inc=True)
        hq_all = hq_done[(NIB - 1, 1)]
        wqd_all = wqd_cv[NIB - 1]

        def wqdblk(ib, hb):
            return sb_big[:, WQUC + ib * 1024 + hb * 128:
                          WQUC + ib * 1024 + (hb + 1) * 128]

        x1_last = 0
        rhs_cv = {}
        down_done = {}
        for tt in range(4):
            for ib in range(NIB):
                slot = (ib % 4) * 512
                w = [(ve, hq_done[(ib, tt // 2)]), (se, wqd_cv[ib])]
                if ib >= 4 or tt > 0:
                    pk = (tt, ib - 4) if ib >= 4 else \
                        (tt - 1, [20, 21, 18, 19][ib])
                    w.append((te, down_done.get(pk, 0)))
                rhs_cv[(tt, ib)] = sop(
                    lambda e, ib=ib, tt=tt, slot=slot: e.activation(
                        out=sb_qt[:, slot:slot + 512],
                        in_=big8[:, ib * 2048 + tt * 512:
                                 ib * 2048 + (tt + 1) * 512],
                        func=ACT.Copy), waits=w, inc=True)
                w = [(se, rhs_cv[(tt, ib)])]
                if ib == 0 and tt > 0:
                    w.append((ve, evac2_done[tt - 1]))
                for hb in range(NHB):
                    down_done[(tt, ib)] = top(
                        lambda e, ib=ib, hb=hb, slot=slot: e.matmul(
                            psb[hb][:, 0:512], wqdblk(ib, hb),
                            sb_qt[:, slot:slot + 512],
                            start=(ib == 0), stop=(ib == NIB - 1)),
                        waits=w if hb == 0 else [], inc=(hb == NHB - 1))
            dd = down_done[(tt, NIB - 1)]
            ev = 0
            for hb in range(NHB):
                wx = [(ve, ev)] if hb > 0 else \
                     [(ve, evac2_done.get(tt - 1, 0)), (te, dd)]
                ldx = dma("sync", dp3[0], sb_tmp[:, 0:512],
                          d_in["xT"][hb * 128:(hb + 1) * 128,
                                     tt * 512:(tt + 1) * 512], waits=wx)
                lda = dma("sync", dp3[1], sb_tmp[:, 512:1024],
                          scr_att0[hb * 128:(hb + 1) * 128,
                                   tt * 512:(tt + 1) * 512],
                          waits=[(ve, ev)] if hb > 0 else
                          [(dsta, p1_st), (ve, evac2_done.get(tt - 1, 0))])
                vop(lambda e, hb=hb: e.tensor_scalar(
                    out=sb_tmp[:, 1024:1536], in0=psb[hb][:, 0:512],
                    scalar1=sb_sm[:, SM_DS + hb:SM_DS + hb + 1], scalar2=None,
                    op0=ALU.mult),
                    waits=[(te, dd), (ve, v_combd), (dsta, x1_last)])
                vop(lambda e: e.tensor_tensor(
                    out=sb_tmp[:, 1024:1536], in0=sb_tmp[:, 1024:1536],
                    in1=sb_tmp[:, 0:512], op=ALU.add),
                    waits=[(dp3[0], ldx)])
                ev = vop(lambda e: e.tensor_tensor(
                    out=sb_tmp[:, 1536:2048], in0=sb_tmp[:, 1024:1536],
                    in1=sb_tmp[:, 512:1024], op=ALU.add),
                    waits=[(dp3[1], lda)], inc=True)
                tgt2 = outT if dbg == "x1" else scr_x1
                x1_last = dma(
                    "gpsimd", dsta,
                    tgt2[hb * 128:(hb + 1) * 128, tt * 512:(tt + 1) * 512],
                    sb_tmp[:, 1536:2048], waits=[(ve, ev)])
            evac2_done[tt] = ev
        p2_ve = evac2_done[3]
        p2_st = x1_last

        if upto == 2:
            p.waits("gpsimd", [(dsta, p2_st), (ve, p2_ve)])
            _close(p, nc)
            return nc

        ln1_psum_free = {}
        xn1_done = {}
        x1mm, x1sq, x1sqmm = {}, {}, {}
        for tt in range(4):
            for h in range(NHB):
                slot = slot_seq % 2
                slot_seq += 1
                w = [(te, x1mm.get((tt, h - 2), 0)),
                     (se, x1sq.get((tt - 1, h - 2 + NHB), 0))]
                if tt == 0 and h == 0:
                    w += [(dsta, p2_st), (ve, p2_ve)]
                if h < 2:
                    w.append((ve, xn1_done.get((NHB - 1, tt - 1), 0)))
                ld = dma("sync", dx[slot], sb_s32[:, slot * 512:(slot + 1) * 512],
                         scr_x1[h * 128:(h + 1) * 128, tt * 512:(tt + 1) * 512],
                         waits=w)
                x1mm[(tt, h)] = top(lambda e, slot=slot, tt=tt, h=h: e.matmul(
                    psb[2 * (tt % 2)][0:1, 0:512], ones_f32,
                    sb_s32[:, slot * 512:(slot + 1) * 512],
                    start=(h == 0), stop=(h == NHB - 1)),
                    waits=[(dx[slot], ld)] +
                          ([(ve, ln1_psum_free[tt - 2])] if (tt >= 2 and
                            h == 0) else []), inc=True)
            for h in range(NHB):
                slot = slot_seq % 2
                slot_seq += 1
                ld = dma("sync", dx[slot], sb_s32[:, slot * 512:(slot + 1) * 512],
                         scr_x1[h * 128:(h + 1) * 128, tt * 512:(tt + 1) * 512],
                         waits=[(se, x1sq.get((tt, h - 2), 0)),
                                (te, x1mm.get((tt, h - 2 + NHB), 0))])
                x1sq[(tt, h)] = sop(lambda e, slot=slot: e.activation(
                    out=sb_s32[:, 1024:1536],
                    in_=sb_s32[:, slot * 512:(slot + 1) * 512],
                    func=ACT.Square),
                    waits=[(dx[slot], ld), (te, x1sqmm.get((tt, h - 1), 0))],
                    inc=True)
                x1sqmm[(tt, h)] = top(lambda e, tt=tt, h=h: e.matmul(
                    psb[2 * (tt % 2) + 1][0:1, 0:512], ones_f32,
                    sb_s32[:, 1024:1536],
                    start=(h == 0), stop=(h == NHB - 1)),
                    waits=[(se, x1sq[(tt, h)])], inc=True)
            mm_t = max(x1mm[(tt, NHB - 1)], x1sqmm[(tt, NHB - 1)])
            ROW_A = sb_sm[0:1, SM_BCA:SM_BCA + 512]
            ROW_B = sb_sm[0:1, SM_BCB:SM_BCB + 512]
            ROW_C = sb_tmp[0:1, 1024:1536]
            vop(lambda e, ROW_A=ROW_A, tt=tt: e.tensor_scalar(
                out=ROW_A, in0=psb[2 * (tt % 2)][0:1, 0:512],
                scalar1=1.0 / H, scalar2=None, op0=ALU.mult),
                waits=[(te, mm_t)] + ([(ve, xn1_done.get((NHB - 1, tt - 1), 0))]
                                      if tt else []))
            ln1_psum_free[tt] = vop(lambda e, ROW_B=ROW_B, tt=tt: e.tensor_scalar(
                out=ROW_B, in0=psb[2 * (tt % 2) + 1][0:1, 0:512],
                scalar1=1.0 / H, scalar2=None, op0=ALU.mult), inc=True)
            vop(lambda e, ROW_A=ROW_A, ROW_C=ROW_C: e.tensor_tensor(
                out=ROW_C, in0=ROW_A, in1=ROW_A, op=ALU.mult))
            v_var = vop(lambda e, ROW_B=ROW_B, ROW_C=ROW_C: e.tensor_tensor(
                out=ROW_C, in0=ROW_B, in1=ROW_C, op=ALU.subtract), inc=True)
            s_sd = sop(lambda e, ROW_B=ROW_B, ROW_C=ROW_C: e.activation(
                out=ROW_B, in_=ROW_C, func=ACT.Sqrt, bias=EPS),
                waits=[(ve, v_var)], inc=True)
            vop(lambda e, ROW_B=ROW_B, ROW_C=ROW_C: e.reciprocal(
                out=ROW_C, in_=ROW_B), waits=[(se, s_sd)])
            vop(lambda e, ROW_A=ROW_A, ROW_C=ROW_C: e.tensor_tensor(
                out=ROW_A, in0=ROW_A, in1=ROW_C, op=ALU.mult))
            v_rows = vop(lambda e, ROW_A=ROW_A: e.tensor_scalar(
                out=ROW_A, in0=ROW_A, scalar1=-1.0, scalar2=None,
                op0=ALU.mult), inc=True)
            bs = bc_slot(1024)
            st1 = dma("gpsimd", dbcs[0], scr_bc[0:1, bs:bs + 512],
                      ROW_C, waits=[(ve, v_rows)])
            st2 = dma("gpsimd", dbcs[1], scr_bc[0:1, bs + 512:bs + 1024],
                      ROW_A)
            ld1 = dma("gpsimd", dbcl[0], sb_sm[:, SM_BCA:SM_BCA + 512],
                      scr_bc[0:1, bs:bs + 512].broadcast_to([128, 512]),
                      waits=[(dbcs[0], st1), (dbcs[1], st2)])
            ld2 = dma("gpsimd", dbcl[1], sb_sm[:, SM_BCB:SM_BCB + 512],
                      scr_bc[0:1, bs + 512:bs + 1024].broadcast_to([128, 512]),
                      waits=[])
            for h in range(NHB):
                ia = tt * NHB + h
                slot = slot_seq % 2
                slot_seq += 1
                xt_ = sb_xn[:, h * T + tt * 512:h * T + (tt + 1) * 512]
                t1 = sb_s32[:, 1024:1536]
                ld = dma("sync", dx[slot], sb_s32[:, slot * 512:(slot + 1) * 512],
                         scr_x1[h * 128:(h + 1) * 128, tt * 512:(tt + 1) * 512],
                         waits=[(ve, xn1_done.get(((ia - 2) % NHB,
                                                   (ia - 2) // NHB), 0)
                                 if ia >= 2 else 0),
                                (se, x1sq[(tt, NHB - 1)]) if h < 2 else (ve, 0)])
                vop(lambda e, slot=slot, t1=t1: e.tensor_tensor(
                    out=t1, in0=sb_s32[:, slot * 512:(slot + 1) * 512],
                    in1=sb_sm[:, SM_BCA:SM_BCA + 512], op=ALU.mult),
                    waits=[(dx[slot], ld)] +
                          ([(dbcl[0], ld1), (dbcl[1], ld2)] if h == 0 else []))
                vop(lambda e, t1=t1: e.tensor_tensor(
                    out=t1, in0=t1, in1=sb_sm[:, SM_BCB:SM_BCB + 512],
                    op=ALU.add))
                xn1_done[(h, tt)] = vop(lambda e, xt_=xt_, t1=t1: e.tensor_copy(
                    out=xt_, in_=t1), inc=True)
            p.waits("gpsimd", [(ve, xn1_done[(NHB - 1, tt)])])
        xn1_all = xn1_done[(NHB - 1, 3)]

        def cast_w(wap, wait_extra=()):
            done = 0
            for hb in range(NHB):
                stg = sb_s32[:, 1536:2560]
                w = [(ve, done)] if hb else list(wait_extra)
                ld = dma("sync", dqs[0], stg, wap[hb * 128:(hb + 1) * 128, :],
                         waits=w)
                done = vop(lambda e, hb=hb, stg=stg: e.tensor_copy(
                    out=sb_str[:, hb * 1024:(hb + 1) * 1024], in_=stg),
                    waits=[(dqs[0], ld)], inc=True)
            return done

        def wblk1(hb, db):
            return sb_str[:, hb * 1024 + db * 128:hb * 1024 + (db + 1) * 128]

        QTC, KTC, VAC, HARENA = 0, 16384, 32768, 49152
        cw = cast_w(d_in["L1_attn_W_Q"], wait_extra=[(ve, xn1_all)])
        qkv_ev = 0
        last_mm = {}
        for dst, wname in ((QTC, "L1_attn_W_Q"), (KTC, "L1_attn_W_K")):
            if dst == KTC:
                cw = cast_w(d_in["L1_attn_W_K"], wait_extra=[(te, last_mm[QTC]),
                                                             (ve, qkv_ev)])
            for tt in range(4):
                for db in range(NHB):
                    pbk = (tt * NHB + db) % 2
                    for hb in range(NHB):
                        last_mm[dst] = top(
                            lambda e, tt=tt, db=db, hb=hb, pbk=pbk: e.matmul(
                                psb[pbk][:, 0:512], wblk1(hb, db),
                                sb_xn[:, hb * T + tt * 512:hb * T + (tt + 1) * 512],
                                start=(hb == 0), stop=(hb == NHB - 1)),
                            waits=([(ve, cw), (ve, qkv_ev)] if hb == 0 else []),
                            inc=(hb == NHB - 1))
                    qkv_ev = vop(lambda e, dst=dst, tt=tt, db=db, pbk=pbk:
                                 e.tensor_copy(
                                     out=sb_big[:, dst + db * T + tt * 512:
                                                dst + db * T + (tt + 1) * 512],
                                     in_=psb[pbk][:, 0:512]),
                                 waits=[(te, last_mm[dst])], inc=True)
        cw = cast_w(d_in["L1_attn_W_V"], wait_extra=[(te, last_mm[KTC]),
                                                     (ve, qkv_ev)])
        for tb in range(16):
            for dh in range(2):
                pbk = (tb * 2 + dh) % 2
                for hb in range(NHB):
                    last_mm[VAC] = top(
                        lambda e, tb=tb, dh=dh, hb=hb, pbk=pbk: e.matmul(
                            psb[pbk][:, 0:512],
                            sb_xn[:, hb * T + tb * 128:hb * T + (tb + 1) * 128],
                            sb_str[:, hb * 1024 + dh * 512:hb * 1024 + (dh + 1) * 512],
                            start=(hb == 0), stop=(hb == NHB - 1)),
                        waits=([(ve, cw), (ve, qkv_ev)] if hb == 0 else []),
                        inc=(hb == NHB - 1))
                qkv_ev = vop(lambda e, tb=tb, dh=dh, pbk=pbk: e.tensor_copy(
                    out=sb_big[:, VAC + tb * 1024 + dh * 512:
                               VAC + tb * 1024 + (dh + 1) * 512],
                    in_=psb[pbk][:, 0:512]),
                    waits=[(te, last_mm[VAC])], inc=True)
        kv_ready = qkv_ev
        st_k = dma("gpsimd", dcc, ag_in[0:H * T].rearrange("(p c) -> p c", p=128),
                   sb_big[:, KTC:KTC + 16384], waits=[(ve, kv_ready)])
        st_v = dma("gpsimd", dcc, ag_in[H * T:].rearrange("(p c) -> p c", p=128),
                   sb_big[:, VAC:VAC + 16384], waits=[])
        p.emit("gpsimd", lambda e: e.collective_compute(
            "AllGather", ALU.bypass,
            replica_groups=[[0, 1], [2, 3], [4, 5], [6, 7]],
            ins=[ag_in], outs=[ag_out]), waits=[(dcc, st_v)], incs=[(ccm, 1)])
        ar_ag = ccm.n

        ldb_ = dma("sync", dmisc, sb_sm[:, SM_GS:SM_GS + 8],
                   d_in["L1_ffn_bias"])
        bias_ld = dmisc.n
        ffn_sweep_te = {}
        ffn_ev = {}
        prev_ev = qkv_ev
        for sw, wname in enumerate(("L1_ffn_weight", "L1_ffn_aux_weight1",
                                    "L1_ffn_aux_weight2")):
            cw = cast_w(d_in[wname], wait_extra=[(te, last_mm[VAC]),
                                                 (ve, prev_ev)])
            for tt in range(4):
                mm_last = 0
                for db in range(NHB):
                    for hb in range(NHB):
                        src_ = (sb_xn[:, hb * T + tt * 512:
                                      hb * T + (tt + 1) * 512] if sw == 0 else
                                sb_big[:, HARENA + hb * T + tt * 512:
                                       HARENA + hb * T + (tt + 1) * 512])
                        mm_last = top(
                            lambda e, db=db, hb=hb, src_=src_: e.matmul(
                                psb[db][:, 0:512], wblk1(hb, db), src_,
                                start=(hb == 0), stop=(hb == NHB - 1)),
                            waits=([(ve, cw),
                                    ffn_ev.get((sw - 1, tt), (ve, 0)),
                                    ffn_ev.get((sw, tt - 1), (ve, 0)),
                                    (ve, prev_ev if (sw == 0 and tt == 0 and
                                                     db == 0) else 0)]
                                   if hb == 0 else []),
                            inc=(hb == NHB - 1))
                ffn_sweep_te[(sw, tt)] = mm_last
                ev = 0
                for db in range(NHB):
                    dstc = HARENA + db * T + tt * 512
                    if sw < 2:
                        ev = vop(lambda e, db=db, dstc=dstc: e.tensor_copy(
                            out=sb_big[:, dstc:dstc + 512], in_=psb[db][:, 0:512]),
                            waits=[(te, mm_last)] if db == 0 else [], inc=True)
                        ffn_ev[(sw, tt)] = (ve, ev)
                    else:
                        ev = sop(lambda e, db=db, dstc=dstc: e.activation(
                            out=sb_big[:, dstc:dstc + 512], in_=psb[db][:, 0:512],
                            func=ACT.Tanh,
                            bias=sb_sm[:, SM_GS + db:SM_GS + db + 1]),
                            waits=([(te, mm_last), (dmisc, bias_ld)]
                                   if db == 0 else []), inc=True)
                        ffn_ev[(sw, tt)] = (se, ev)
            prev_ev_pair = [ffn_ev[(sw, tt2)] for tt2 in range(4)]
            prev_ev = max(t for (_s, t) in prev_ev_pair)
        q_mv = 0
        for hb in range(NHB):
            q_mv = vop(lambda e, hb=hb: e.tensor_copy(
                out=sb_xn[:, hb * T:(hb + 1) * T],
                in_=sb_big[:, QTC + hb * T:QTC + (hb + 1) * T]),
                waits=[(te, ffn_sweep_te[(0, 3)])] if hb == 0 else [],
                inc=True)
        ffn1_st = {}
        for tt in range(4):
            ffn1_st[tt] = dma(
                "gpsimd", dsth[tt % 2],
                scr_ffn1[:, tt * 512:(tt + 1) * 512].rearrange(
                    "(db p) c -> p db c", p=128),
                sb_big[:, HARENA:HARENA + 16384].rearrange(
                    "p (db c) -> p db c", c=T)[:, :, tt * 512:(tt + 1) * 512],
                waits=[ffn_ev[(2, tt)]])

        if upto == 3:
            p.waits("gpsimd", [(dsth[0], ffn1_st[2]), (dsth[1], ffn1_st[3]),
                               (ve, q_mv)])
            _close(p, nc)
            return nc

        KA, KB, VA, VB = 0, 16384, 32768, 49152
        kvld = {}
        kvsem = {}
        for idx, (dst, shard, off) in enumerate((
                (KA, 0, 0), (KB, 1, 0), (VA, 0, H * T), (VB, 1, H * T))):
            w = [(ccm, ar_ag)]
            if dst in (KA, KB):
                w.append((ve, q_mv))
            else:
                w.append(ffn_ev[(2, 3)])
                if dst == VB:
                    w.append((dsth[0], ffn1_st[2]))
                    w.append((dsth[1], ffn1_st[3]))
            kvld[dst] = dma("sync", dkv[idx],
                            sb_big[:, dst:dst + 16384],
                            ag_out[shard, off:off + H * T].rearrange(
                                "(p c) -> p c", p=128), waits=w)
            kvsem[dst] = dkv[idx]

        SM_NMB, SM_TB, SM_OLD, SM_CR2, SM_L2, SM_RL2 = 6, 7, 8, 9, 10, 11
        out_st = {}
        ep_done = {}
        x1_ld = {}
        f1_ld = {}
        mk_ld = {}
        pt_ev = {}
        pvk = {}
        slab_l = {}
        s_done = {}
        p_done = {}
        pv_done = {}
        resc_done = {}
        for i in range(nq):
            x1_ld[i] = dma("sync", dp3[i % 2],
                           sb_s32[:, (i % 2) * 1024:(i % 2 + 1) * 1024]
                           .rearrange("p (db c) -> p db c", c=128),
                           scr_x1[:, i * 128:(i + 1) * 128].rearrange(
                               "(db p) c -> p db c", p=128),
                           waits=[(ve, ep_done.get(i - 2, 0))])
            f1_ld[i] = dma("scalar", dnat[i % 2],
                           sb_str[:, (i % 2) * 1024:(i % 2 + 1) * 1024]
                           .rearrange("p (db c) -> p db c", c=128),
                           scr_ffn1[:, i * 128:(i + 1) * 128].rearrange(
                               "(db p) c -> p db c", p=128),
                           waits=[(ve, ep_done.get(i - 2, 0)),
                                  (dsth[0], ffn1_st[2]),
                                  (dsth[1], ffn1_st[3])] if i < 2 else
                           [(ve, ep_done.get(i - 2, 0))])
            mk_ld[i] = dma("scalar", dwt[i % 2],
                           sb_s32[:, 2048 + (i % 2) * 128:
                                  2048 + (i % 2 + 1) * 128],
                           d_in["maskA"][i * 128:(i + 1) * 128, :],
                           waits=[(ve, resc_done.get((i - 2, "mA"), 0))])
            mk_ld[(i, "B")] = dma("scalar", dwt[2 + i % 2],
                                  sb_s32[:, 2304 + (i % 2) * 128:
                                         2304 + (i % 2 + 1) * 128],
                                  d_in["maskB"][i * 128:(i + 1) * 128, :],
                                  waits=[(ve, resc_done.get((i - 2, "mB"), 0))])
            slabs = []
            for (tag, base, vbase) in (("A", KA, VA), ("B", KB, VB)):
                nblk = i + 1
                c = 0
                while c < nblk:
                    wblk_ = min(4, nblk - c)
                    slabs.append((tag, base, vbase, c, wblk_,
                                  c + wblk_ == nblk))
                    c += wblk_
            if dbg == "slab1":
                slabs = slabs[:1]
            first = True
            kb_count = 0
            sidx = 0
            for (tag, base, vbase, c0, nb, has_last) in slabs:
                wcols = nb * 128
                sbk = sidx % 2
                sidx += 1
                kkey = KB if tag == "B" else KA
                w = [(kvsem[kkey], kvld[kkey]),
                     (ve, slab_l.get((i, sidx - 3), 0)),
                     (se, p_done.get((i, sidx - 3), 0))]
                if i == 0 and sidx == 1:
                    w.append((ve, q_mv))
                    w.append(ffn_ev[(2, 3)])
                    w.append(ffn_ev[(2, 2)])
                for db in range(NHB):
                    s_done[(i, sidx - 1)] = top(
                        lambda e, i=i, db=db, base=base, c0=c0, wcols=wcols,
                        sbk=sbk: e.matmul(
                            psb[sbk][:, 0:wcols],
                            sb_xn[:, db * T + i * 128:db * T + (i + 1) * 128],
                            sb_big[:, base + db * T + c0 * 128:
                                   base + db * T + c0 * 128 + wcols],
                            start=(db == 0), stop=(db == NHB - 1)),
                        waits=w if db == 0 else [], inc=(db == NHB - 1))
                if has_last:
                    mcol = (2048 if tag == "A" else 2304) + (i % 2) * 128
                    resc_done[(i, "m" + tag)] = vop(
                        lambda e, sbk=sbk, wcols=wcols, mcol=mcol: e.tensor_tensor(
                            out=psb[sbk][:, wcols - 128:wcols],
                            in0=psb[sbk][:, wcols - 128:wcols],
                            in1=sb_s32[:, mcol:mcol + 128], op=ALU.add),
                        waits=[(te, s_done[(i, sidx - 1)]),
                               ((dwt[i % 2], mk_ld[i]) if tag == "A" else
                                (dwt[2 + i % 2], mk_ld[(i, "B")]))],
                        inc=True)
                vop(lambda e, sbk=sbk, wcols=wcols: e.tensor_reduce(
                    out=sb_sm[:, SM_TB:SM_TB + 1], in_=psb[sbk][:, 0:wcols],
                    axis=AX, op=ALU.max, negate=True),
                    waits=[(te, s_done[(i, sidx - 1)])])
                vop(lambda e: e.tensor_scalar(
                    out=sb_sm[:, SM_TB:SM_TB + 1], in0=sb_sm[:, SM_TB:SM_TB + 1],
                    scalar1=1.0 / 32.0, scalar2=None, op0=ALU.mult))
                if first:
                    v_nmb = vop(lambda e: e.tensor_copy(
                        out=sb_sm[:, SM_NMB:SM_NMB + 1],
                        in_=sb_sm[:, SM_TB:SM_TB + 1]), inc=True)
                else:
                    vop(lambda e: e.tensor_copy(
                        out=sb_sm[:, SM_OLD:SM_OLD + 1],
                        in_=sb_sm[:, SM_NMB:SM_NMB + 1]))
                    vop(lambda e: e.tensor_tensor(
                        out=sb_sm[:, SM_NMB:SM_NMB + 1],
                        in0=sb_sm[:, SM_NMB:SM_NMB + 1],
                        in1=sb_sm[:, SM_TB:SM_TB + 1], op=ALU.min))
                    v_nmb = vop(lambda e: e.tensor_tensor(
                        out=sb_sm[:, SM_CR2:SM_CR2 + 1],
                        in0=sb_sm[:, SM_NMB:SM_NMB + 1],
                        in1=sb_sm[:, SM_OLD:SM_OLD + 1], op=ALU.subtract),
                        inc=True)
                    s_cr = sop(lambda e: e.activation(
                        out=sb_sm[:, SM_CR2:SM_CR2 + 1],
                        in_=sb_sm[:, SM_CR2:SM_CR2 + 1], func=ACT.Exp),
                        waits=[(ve, v_nmb)], inc=True)
                p_done[(i, sidx - 1)] = sop(
                    lambda e, sbk=sbk, wcols=wcols: e.activation(
                        out=sb_qt[:, 0:wcols], in_=psb[sbk][:, 0:wcols],
                        func=ACT.Exp, scale=1.0 / 32.0,
                        bias=sb_sm[:, SM_NMB:SM_NMB + 1]),
                    waits=[(ve, v_nmb)], inc=True)
                vop(lambda e, wcols=wcols: e.tensor_reduce(
                    out=sb_sm[:, SM_TB:SM_TB + 1], in_=sb_qt[:, 0:wcols],
                    axis=AX, op=ALU.add), waits=[(se, p_done[(i, sidx - 1)])])
                if first:
                    v_l = vop(lambda e: e.tensor_copy(
                        out=sb_sm[:, SM_L2:SM_L2 + 1],
                        in_=sb_sm[:, SM_TB:SM_TB + 1]), inc=True)
                    resc_done[("S", i, sidx - 1)] = v_l
                else:
                    v_l = vop(lambda e: e.scalar_tensor_tensor(
                        out=sb_sm[:, SM_L2:SM_L2 + 1],
                        in0=sb_sm[:, SM_L2:SM_L2 + 1],
                        scalar=sb_sm[:, SM_CR2:SM_CR2 + 1],
                        in1=sb_sm[:, SM_TB:SM_TB + 1],
                        op0=ALU.mult, op1=ALU.add),
                        waits=[(se, s_cr)], inc=True)
                    for bk in (4, 5):
                        resc_done[("S", i, sidx - 1)] = vop(
                            lambda e, bk=bk: e.tensor_scalar(
                                out=psb[bk][:, 0:512], in0=psb[bk][:, 0:512],
                                scalar1=sb_sm[:, SM_CR2:SM_CR2 + 1],
                                scalar2=None, op0=ALU.mult),
                            waits=[(te, pv_done.get((i, sidx - 2), 0)),
                                   (se, s_cr)],
                            inc=True)
                slab_l[(i, sidx - 1)] = v_l
                for kb in range(nb):
                    pslot = 512 + (kb_count % 2) * 128
                    tp_t = top(lambda e, kb=kb: e.transpose(
                        psb[2].bitcast(BF16)[:, 0:128],
                        sb_qt[:, kb * 128:(kb + 1) * 128], idn_bf),
                        waits=([(se, p_done[(i, sidx - 1)]),
                                (se, pt_ev.get((i, kb_count - 1), 0)),
                                (ge, g_con1)] if kb == 0 else
                               [(se, pt_ev.get((i, kb_count - 1), 0))]),
                        inc=True)
                    pt_ev[(i, kb_count)] = sop(
                        lambda e, pslot=pslot: e.activation(
                            out=sb_qt[:, pslot:pslot + 128],
                            in_=psb[2].bitcast(BF16)[:, 0:128],
                            func=ACT.Copy),
                        waits=[(te, tp_t),
                               (te, pvk.get((i, kb_count - 2), 0))], inc=True)
                    for dh in range(2):
                        pv_done[(i, sidx - 1)] = top(
                            lambda e, vbase=vbase, c0=c0, kb=kb, dh=dh,
                            pslot=pslot, first=first: e.matmul(
                                psb[4 + dh][:, 0:512],
                                sb_qt[:, pslot:pslot + 128],
                                sb_big[:, vbase + (c0 + kb) * 1024 + dh * 512:
                                       vbase + (c0 + kb) * 1024 + (dh + 1) * 512],
                                start=(first and kb == 0), stop=False,
                                skip_group_check=True),
                            waits=(([(se, pt_ev[(i, kb_count)]),
                                     (ve, resc_done.get(("S", i, sidx - 1), 0)),
                                     (kvsem[vbase], kvld[vbase])] +
                                    ([ffn_ev[(2, 0)], ffn_ev[(2, 1)]]
                                     if (i == 0 and first and kb == 0)
                                     else []))
                                   if dh == 0 else []), inc=(dh == 1))
                    pvk[(i, kb_count)] = pv_done[(i, sidx - 1)]
                    kb_count += 1
                first = False
            vop(lambda e: e.reciprocal(
                out=sb_sm[:, SM_RL2:SM_RL2 + 1], in_=sb_sm[:, SM_L2:SM_L2 + 1]),
                waits=[(te, pv_done[(i, sidx - 1)])])
            att_ev = 0
            for bk in (4, 5):
                att_ev = vop(lambda e, bk=bk: e.tensor_scalar(
                    out=sb_qt[:, 1024 + (bk - 4) * 512:1024 + (bk - 3) * 512],
                    in0=psb[bk][:, 0:512],
                    scalar1=sb_sm[:, SM_RL2:SM_RL2 + 1], scalar2=None,
                    op0=ALU.mult), inc=True)
            ep = 0
            ep1 = 0
            for db in range(NHB):
                tp = top(lambda e, db=db: e.transpose(
                    psb[6].bitcast(BF16)[:, 0:128],
                    sb_qt[:, 1024 + db * 128:1152 + db * 128], idn_bf),
                    waits=[(ve, att_ev), (ve, ep1),
                           (ve, ep_done.get(i - 1, 0) if db == 0 else 0)],
                    inc=True)
                ep1 = vop(lambda e, db=db, i=i: e.tensor_tensor(
                    out=sb_tmp[:, (i % 2) * 1024 + db * 128:
                               (i % 2) * 1024 + (db + 1) * 128],
                    in0=psb[6].bitcast(BF16)[:, 0:128],
                    in1=sb_s32[:, (i % 2) * 1024 + db * 128:
                               (i % 2) * 1024 + (db + 1) * 128],
                    op=ALU.add), waits=[(te, tp), (dp3[i % 2], x1_ld[i]),
                                        (dou[i % 2], out_st.get(i - 2, 0))],
                    inc=True)
                ep = vop(lambda e, db=db, i=i: e.tensor_tensor(
                    out=sb_tmp[:, (i % 2) * 1024 + db * 128:
                               (i % 2) * 1024 + (db + 1) * 128],
                    in0=sb_tmp[:, (i % 2) * 1024 + db * 128:
                               (i % 2) * 1024 + (db + 1) * 128],
                    in1=sb_str[:, (i % 2) * 1024 + db * 128:
                               (i % 2) * 1024 + (db + 1) * 128],
                    op=ALU.add), waits=[(dnat[i % 2], f1_ld[i])], inc=True)
            ep_done[i] = ep
            out_st[i] = dma("gpsimd", dou[i % 2],
                            outT[:, i * 128:(i + 1) * 128].rearrange(
                                "(db p) c -> p db c", p=128),
                            sb_tmp[:, (i % 2) * 1024:(i % 2 + 1) * 1024]
                            .rearrange("p (db c) -> p db c", c=128),
                            waits=[(ve, ep)])
        p.waits("gpsimd", [(dou[i2 % 2], out_st[i2])
                           for i2 in (range(nq - 2, nq) if nq > 1 else [0])])


        _close(p, nc)
        return nc


def _close(p, nc):
    with nc.Block() as block:
        for ename in ENGINES:
            def make(ename):
                def f(e):
                    for fn in p.st[ename]:
                        fn(e)
                return f
            getattr(block, ename)(make(ename))


def _pack_core(x, c):
    b, pp = c // 2, c % 2
    rows = np.concatenate([x[b, j * 128:(j + 1) * 128, :] for j in OWN[pp]],
                          axis=0)
    return np.ascontiguousarray(rows.T)


def _unpack(outs):
    out = np.empty((B, S, H), np.float32)
    for c in range(8):
        b, pp = c // 2, c % 2
        rows = np.ascontiguousarray(outs[c].T)
        for kk, j in enumerate(OWN[pp]):
            out[b, j * 128:(j + 1) * 128, :] = rows[kk * 128:(kk + 1) * 128, :]
    return out


def make_in_maps(inputs):
    base = {}
    for k, v in inputs.items():
        if k == "L0_attn_x":
            continue
        base[k] = np.ascontiguousarray(np.asarray(v), dtype=np.float32)
    base["L0_ffn_clip_x"] = base["L0_ffn_clip_x"].reshape(1, 1)
    base["L0_ffn_clip_down"] = base["L0_ffn_clip_down"].reshape(1, 1)
    base["L1_ffn_bias"] = np.ascontiguousarray(
        base["L1_ffn_bias"].reshape(H).reshape(8, 128).T)
    base["wgT"] = np.ascontiguousarray(base["L0_ffn_w_gate"].T)
    base["wuT"] = np.ascontiguousarray(base["L0_ffn_w_up"].T)
    base["wdT"] = np.ascontiguousarray(base["L0_ffn_w_down"].T)
    x = np.asarray(inputs["L0_attn_x"])
    tri = np.where(np.tril(np.ones((128, 128), bool)), 0.0, -1.0e9)
    tri = tri.astype(np.float32)
    zeros = np.zeros((128, 128), np.float32)
    neg = np.full((128, 128), -1.0e9, np.float32)
    masks = {}
    for pp in range(2):
        own, other = OWN[pp], OWN[1 - pp]
        mA = np.empty((16, 128, 128), np.float32)
        mB = np.empty((16, 128, 128), np.float32)
        for i in range(16):
            g = own[i]
            gA, gB = OWN[0][i], OWN[1][i]
            for (mm, gx) in ((mA, gA), (mB, gB)):
                if gx == g:
                    mm[i] = tri
                elif gx < g:
                    mm[i] = zeros
                else:
                    mm[i] = neg
        masks[pp] = (np.ascontiguousarray(mA.reshape(16 * 128, 128)),
                     np.ascontiguousarray(mB.reshape(16 * 128, 128)))
    in_maps = []
    for c in range(8):
        m = dict(base)
        m["xT"] = _pack_core(x, c)
        m["maskA"], m["maskB"] = masks[c % 2]
        in_maps.append(m)
    return in_maps


def kernel(**inputs):
    nc = build_program(upto=4)
    in_maps = make_in_maps(inputs)
    res = run_bass_kernel_spmd(nc, in_maps, core_ids=list(range(8)))
    return _unpack([res.results[c]["outT"] for c in range(8)])
